# version 02 (final)
# baseline (speedup 1.0000x reference)
import math
import numpy as np
import ml_dtypes

import concourse.bass as bass
import concourse.mybir as mybir
from concourse.bass_utils import run_bass_kernel_spmd

F32 = mybir.dt.float32
BF16 = mybir.dt.bfloat16
AF = mybir.ActivationFunctionType
ALU = mybir.AluOpType

NCORES = 8
BF = ml_dtypes.bfloat16
NEG = -1e30
SCALE = 1.0 / math.sqrt(128.0)

def cdiv(a, b):
    return (a + b - 1) // b

def padk(a, kpad):
    a = np.asarray(a, np.float32)
    k, m = a.shape
    if k == kpad:
        return np.ascontiguousarray(a)
    out = np.zeros((kpad, m), np.float32)
    out[:k] = a
    return out

ENGS = ("sync", "tensor", "vector", "scalar", "gpsimd")

class Bld:
    def __init__(self, nc):
        self.nc = nc
        self.items = {e: [] for e in ENGS}
        self.cnt = {}
        self.sems = {}
        self.waited = {e: {} for e in ENGS}

    def op(self, eng, fn, deps=(), dma=False):
        sem = eng + ("d" if dma else "c")
        inc = 16 if dma else 1
        w = []
        for t in deps:
            if t is None:
                continue
            s, v = t
            prev = self.waited[eng].get(s, -1)
            if v > prev:
                w.append((s, v))
                self.waited[eng][s] = v
        val = self.cnt.get(sem, 0) + inc
        self.cnt[sem] = val
        self.items[eng].append((w, fn, sem, inc))
        return (sem, val)

    def dma(self, eng, out, in_, deps=()):
        return self.op(eng, lambda e, o=out, i=in_: e.dma_start(out=o, in_=i), deps, dma=True)

    def set_guard(self, ap):
        self._guard = ap

    def touch(self, eng, deps=()):
        if eng == "scalar":
            return self.op(eng, lambda e: e.copy(self._guard[0:1, 0:1], self._guard[0:1, 1:2]), deps)
        return self.op(eng, lambda e: e.tensor_copy(self._guard[0:1, 0:1], self._guard[0:1, 1:2]), deps)

    def dma_out(self, eng, out, in_, prod_tok, prod_eng, extra=()):
        t = self.touch(prod_eng, deps=[prod_tok])
        return self.dma(eng, out, in_, deps=[prod_tok, t] + list(extra))

    def finalize(self):
        nc = self.nc
        import contextlib
        stack = contextlib.ExitStack()
        for n in sorted(self.cnt.keys()):
            self.sems[n] = stack.enter_context(nc.semaphore(n))
        with stack, nc.Block() as block:
            for eng in ENGS:
                if not self.items[eng]:
                    continue
                def mk(engname):
                    def body(engobj):
                        for (w, fn, sem, inc) in self.items[engname]:
                            for (s, v) in w:
                                engobj.wait_ge(self.sems[s], v)
                            fn(engobj).then_inc(self.sems[sem], inc)
                    return body
                getattr(block, eng)(mk(eng))
        return nc

    def finish(self, tokens):
        self.op("gpsimd", lambda e: e.engine_nop(), deps=tokens)


_PROG_CACHE = {}

def get_prog(key, builder):
    if key not in _PROG_CACHE:
        _PROG_CACHE[key] = builder()
    return _PROG_CACHE[key]

def runp(nc, in_maps):
    res = run_bass_kernel_spmd(nc, in_maps, core_ids=list(range(NCORES)))
    return res.results

def build_mmT(Kpad, M, N, act):
    nc = bass.Bass("TRN2", target_bir_lowering=False)
    A = nc.dram_tensor("A", [Kpad, M], F32, kind="ExternalInput").ap()
    Bm = nc.dram_tensor("B", [Kpad, N], F32, kind="ExternalInput").ap()
    C = nc.dram_tensor("C", [M, N], F32, kind="ExternalOutput").ap()
    KT = Kpad // 128
    MT = cdiv(M, 128)
    NCH = cdiv(N, 512)
    KG = cdiv(KT, 8)
    Ar = A.rearrange("(kt p) m -> p kt m", p=128)
    Br = Bm.rearrange("(kt p) n -> p kt n", p=128)
    actf = {"none": AF.Copy, "silu": AF.Silu, "relu": AF.Relu, "sigmoid": AF.Sigmoid}[act]

    b = Bld(nc)
    import contextlib
    stack = contextlib.ExitStack()
    mtl = min(MT, 8)
    mwmax = min(M, mtl * 128)
    a32 = stack.enter_context(nc.sbuf_tensor("a32", [128, 8 * mwmax], F32))
    abf = stack.enter_context(nc.sbuf_tensor("abf", [128, 8 * mwmax], BF16))
    b32 = stack.enter_context(nc.sbuf_tensor("b32", [128, 8 * 512], F32))
    bbf = stack.enter_context(nc.sbuf_tensor("bbf", [128, 8 * 512], BF16))
    cev = stack.enter_context(nc.sbuf_tensor("cev", [128, mtl * 512], F32))
    psums = [stack.enter_context(nc.psum_tensor(f"ps{i}", [128, 512], F32)) for i in range(mtl)]
    with stack:
        outtoks = []
        t_mm = None
        t_castA = None
        t_castB = None
        ev_tok = {}
        cev_tok = {}
        for mg in range(cdiv(MT, 8)):
            mts = list(range(mg * 8, min(mg * 8 + 8, MT)))
            mws = [128 if (mt + 1) * 128 <= M else M - mt * 128 for mt in mts]
            moffs = [sum(mws[:i]) for i in range(len(mts))]
            mw = sum(mws)
            m0 = mts[0] * 128
            for nch in range(NCH):
                n0 = nch * 512
                nw = min(512, N - n0)
                for kg in range(KG):
                    kts = list(range(kg * 8, min(kg * 8 + 8, KT)))
                    nk = len(kts)
                    dA = b.dma("sync", a32[:, 0:nk * mw].rearrange("p (kt m) -> p kt m", kt=nk),
                               Ar[:, kts[0]:kts[0] + nk, m0:m0 + mw],
                               deps=[t_castA])
                    t_castA = b.op("vector", lambda e, w=nk * mw: e.tensor_copy(abf[:, 0:w], a32[:, 0:w]),
                                   deps=[dA, t_mm])
                    dB = b.dma("sync", b32[:, 0:nk * nw].rearrange("p (kt n) -> p kt n", kt=nk),
                               Br[:, kts[0]:kts[0] + nk, n0:n0 + nw],
                               deps=[t_castB])
                    t_castB = b.op("gpsimd", lambda e, w=nk * nw: e.tensor_copy(bbf[:, 0:w], b32[:, 0:w]),
                                   deps=[dB, t_mm])
                    for mi in range(len(mts)):
                        mwi, moff = mws[mi], moffs[mi]
                        deps = [t_castA, t_castB]
                        if kg == 0:
                            deps.append(ev_tok.get(mi))
                        for ki in range(nk):
                            st = (kg == 0 and ki == 0)
                            sp = (kg == KG - 1 and ki == nk - 1)
                            t_mm = b.op("tensor",
                                        lambda e, mi=mi, mwi=mwi, moff=moff, ki=ki, nw=nw, st=st, sp=sp:
                                        e.matmul(psums[mi][0:mwi, 0:nw],
                                                 abf[:, ki * mw + moff: ki * mw + moff + mwi],
                                                 bbf[:, ki * nw: ki * nw + nw],
                                                 start=st, stop=sp),
                                        deps=deps)
                            deps = []
                for mi in range(len(mts)):
                    mwi = mws[mi]
                    ev = b.op("scalar", lambda e, mi=mi, mwi=mwi, nw=nw: e.activation(
                        cev[0:mwi, mi * 512: mi * 512 + nw], psums[mi][0:mwi, 0:nw], actf),
                        deps=[t_mm, cev_tok.get(mi)])
                    ev_tok[mi] = ev
                    od = b.dma("gpsimd", C[mts[mi] * 128: mts[mi] * 128 + mwi, n0:n0 + nw],
                               cev[0:mwi, mi * 512: mi * 512 + nw], deps=[ev])
                    cev_tok[mi] = od
                    outtoks.append(od)
        b.finish(outtoks[-max(1, mtl * 2):])
    return b.finalize()

def np_mmT(A, B, act):
    Ab = A.astype(BF).astype(np.float32)
    Bb = B.astype(BF).astype(np.float32)
    C = Ab.T @ Bb
    if act == "silu":
        C = C / (1.0 + np.exp(-C))
    elif act == "relu":
        C = np.maximum(C, 0.0)
    elif act == "sigmoid":
        C = 1.0 / (1.0 + np.exp(-C))
    return C.astype(np.float32)

def build_ln(M, D):
    nc = bass.Bass("TRN2", target_bir_lowering=False)
    X = nc.dram_tensor("X", [M, D], F32, kind="ExternalInput").ap()
    Y = nc.dram_tensor("Y", [M, D], F32, kind="ExternalOutput").ap()
    MT = M // 128
    b = Bld(nc)
    import contextlib
    stack = contextlib.ExitStack()
    x = stack.enter_context(nc.sbuf_tensor("x", [128, D], F32))
    y = stack.enter_context(nc.sbuf_tensor("y", [128, D], F32))
    sq = stack.enter_context(nc.sbuf_tensor("sq", [128, D], F32))
    col = stack.enter_context(nc.sbuf_tensor("col", [128, 8], F32))
    cns = stack.enter_context(nc.sbuf_tensor("cns", [128, 2], F32))
    with stack:
        outtoks = []
        k0 = b.op("vector", lambda e: e.memset(cns[:, 0:1], 1e-5))
        k1 = b.op("vector", lambda e: e.memset(cns[:, 1:2], -0.5), deps=[k0])
        t_prev = k1
        t_out = None
        for mt in range(MT):
            d = b.dma("sync", x[:], X[mt * 128:(mt + 1) * 128, :], deps=[t_prev])
            s1 = b.op("vector", lambda e: e.reduce_sum(col[:, 0:1], x[:], axis=mybir.AxisListType.X), deps=[d])
            s2 = b.op("scalar", lambda e: e.activation(sq[:], x[:], AF.Square, accum_out=col[:, 1:2]), deps=[d, t_prev])
            m1 = b.op("vector", lambda e: e.tensor_scalar(col[:, 2:3], col[:, 0:1], 1.0 / D, None, ALU.mult), deps=[s1])
            m2 = b.op("vector", lambda e: e.tensor_scalar(col[:, 3:4], col[:, 1:2], 1.0 / D, None, ALU.mult), deps=[s2])
            m3 = b.op("vector", lambda e: e.tensor_tensor(col[:, 4:5], col[:, 2:3], col[:, 2:3], ALU.mult), deps=[m1])
            m4 = b.op("vector", lambda e: e.tensor_tensor(col[:, 4:5], col[:, 3:4], col[:, 4:5], ALU.subtract), deps=[m2, m3])
            l1 = b.op("scalar", lambda e: e.activation(col[:, 5:6], col[:, 4:5], AF.Ln, bias=cns[:, 0:1]), deps=[m4])
            l2 = b.op("scalar", lambda e: e.activation(col[:, 6:7], col[:, 5:6], AF.Exp, scale=cns[:, 1:2]), deps=[l1])
            m6 = b.op("vector", lambda e: e.tensor_scalar(col[:, 7:8], col[:, 2:3], -1.0, None, ALU.mult), deps=[m1, l2])
            yy = b.op("vector", lambda e: e.tensor_scalar(y[:], x[:], col[:, 7:8], col[:, 6:7],
                                                          ALU.add, ALU.mult),
                      deps=[m6, t_out])
            t_out = b.dma("gpsimd", Y[mt * 128:(mt + 1) * 128, :], y[:], deps=[yy])
            outtoks.append(t_out)
            t_prev = yy
        b.finish(outtoks[-2:])
    return b.finalize()

def np_ln(X):
    m = X.mean(axis=1, keepdims=True)
    v = X.var(axis=1, keepdims=True)
    return ((X - m) / np.sqrt(v + 1e-5)).astype(np.float32)


def build_muladd(M, N, sub):
    nc = bass.Bass("TRN2", target_bir_lowering=False)
    p = nc.dram_tensor("p", [M, N], F32, kind="ExternalInput").ap()
    q = nc.dram_tensor("q", [M, N], F32, kind="ExternalInput").ap()
    r = nc.dram_tensor("r", [M, N], F32, kind="ExternalInput").ap()
    t = nc.dram_tensor("t", [M, N], F32, kind="ExternalInput").ap()
    o = nc.dram_tensor("o", [M, N], F32, kind="ExternalOutput").ap()
    MT = M // 128
    b_ = Bld(nc)
    import contextlib
    stack = contextlib.ExitStack()
    tp = stack.enter_context(nc.sbuf_tensor("tp", [128, N], F32))
    tq = stack.enter_context(nc.sbuf_tensor("tq", [128, N], F32))
    tr = stack.enter_context(nc.sbuf_tensor("tr", [128, N], F32))
    tt = stack.enter_context(nc.sbuf_tensor("tt", [128, N], F32))
    u = stack.enter_context(nc.sbuf_tensor("u", [128, N], F32))
    w = stack.enter_context(nc.sbuf_tensor("w", [128, N], F32))
    ob = stack.enter_context(nc.sbuf_tensor("ob", [128, N], F32))
    grd = stack.enter_context(nc.sbuf_tensor("grd", [1, 4], F32))
    b_.set_guard(grd)
    op2 = ALU.subtract if sub else ALU.add
    with stack:
        outtoks = []
        guard = []
        for mt in range(MT):
            sl = slice(mt * 128, (mt + 1) * 128)
            d1 = b_.dma("sync", tp[:], p[sl, :], deps=guard)
            d2 = b_.dma("sync", tq[:], q[sl, :], deps=guard)
            d3 = b_.dma("sync", tr[:], r[sl, :], deps=guard)
            d4 = b_.dma("sync", tt[:], t[sl, :], deps=guard)
            m1 = b_.op("vector", lambda e: e.tensor_tensor(u[:], tp[:], tq[:], ALU.mult), deps=[d1, d2, guard[0] if guard else None])
            m2 = b_.op("vector", lambda e: e.tensor_tensor(w[:], tr[:], tt[:], ALU.mult), deps=[d3, d4])
            m3 = b_.op("vector", lambda e, op2=op2: e.tensor_tensor(ob[:], u[:], w[:], op2), deps=[m1, m2, (guard[1] if guard else None)])
            od = b_.dma_out("gpsimd", o[sl, :], ob[:], m3, "vector")
            outtoks.append(od)
            guard = [m3, od]
        b_.finish(outtoks[-2:])
    return b_.finalize()

def np_muladd_sub(p, q, r, t):
    return (p * q - r * t).astype(np.float32)

def np_muladd_add(p, q, r, t):
    return (p * q + r * t).astype(np.float32)

def build_swiglu(M, I):
    nc = bass.Bass("TRN2", target_bir_lowering=False)
    H = nc.dram_tensor("H", [M, 2 * I], F32, kind="ExternalInput").ap()
    O = nc.dram_tensor("O", [M, I], F32, kind="ExternalOutput").ap()
    MT = M // 128
    b = Bld(nc)
    import contextlib
    stack = contextlib.ExitStack()
    g = stack.enter_context(nc.sbuf_tensor("g", [128, I], F32))
    u = stack.enter_context(nc.sbuf_tensor("u", [128, I], F32))
    t = stack.enter_context(nc.sbuf_tensor("t", [128, I], F32))
    with stack:
        outtoks = []
        guard = []
        for mt in range(MT):
            sl = slice(mt * 128, (mt + 1) * 128)
            dg = b.dma("sync", g[:], H[sl, 0:I], deps=guard)
            du = b.dma("sync", u[:], H[sl, I:2 * I], deps=guard)
            s1 = b.op("scalar", lambda e: e.activation(t[:], g[:], AF.Silu), deps=[dg] + guard)
            m1 = b.op("vector", lambda e: e.tensor_tensor(t[:], t[:], u[:], ALU.mult), deps=[s1, du])
            od = b.dma("gpsimd", O[sl, :], t[:], deps=[m1])
            outtoks.append(od)
            guard = [m1, od]
        b.finish(outtoks[-2:])
    return b.finalize()

def np_swiglu(H):
    I = H.shape[1] // 2
    g = H[:, :I]
    return (g / (1.0 + np.exp(-g)) * H[:, I:]).astype(np.float32)

def build_ew3(M, D):
    nc = bass.Bass("TRN2", target_bir_lowering=False)
    X = nc.dram_tensor("X", [M, D], F32, kind="ExternalInput").ap()
    Aa = nc.dram_tensor("Aa", [M, D], F32, kind="ExternalInput").ap()
    Bb = nc.dram_tensor("Bb", [M, D], F32, kind="ExternalInput").ap()
    O = nc.dram_tensor("O", [M, D], F32, kind="ExternalOutput").ap()
    MT = M // 128
    b = Bld(nc)
    import contextlib
    stack = contextlib.ExitStack()
    x = stack.enter_context(nc.sbuf_tensor("x", [128, D], F32))
    aa = stack.enter_context(nc.sbuf_tensor("aa", [128, D], F32))
    bb2 = stack.enter_context(nc.sbuf_tensor("bb2", [128, D], F32))
    with stack:
        outtoks = []
        guard = []
        for mt in range(MT):
            sl = slice(mt * 128, (mt + 1) * 128)
            d1 = b.dma("sync", x[:], X[sl, :], deps=guard)
            d2 = b.dma("sync", aa[:], Aa[sl, :], deps=guard)
            d3 = b.dma("sync", bb2[:], Bb[sl, :], deps=guard)
            m1 = b.op("vector", lambda e: e.tensor_tensor(aa[:], aa[:], bb2[:], ALU.mult), deps=[d2, d3])
            m2 = b.op("vector", lambda e: e.tensor_tensor(x[:], x[:], aa[:], ALU.add), deps=[d1, m1])
            od = b.dma("gpsimd", O[sl, :], x[:], deps=[m2])
            outtoks.append(od)
            guard = [m2, od]
        b.finish(outtoks[-2:])
    return b.finalize()

def np_ew3(X, A, B):
    return (X + A * B).astype(np.float32)

def build_g1silu(M, D):
    nc = bass.Bass("TRN2", target_bir_lowering=False)
    X = nc.dram_tensor("X", [M, D], F32, kind="ExternalInput").ap()
    U = nc.dram_tensor("U", [M, D], F32, kind="ExternalInput").ap()
    O = nc.dram_tensor("O", [M, D], F32, kind="ExternalOutput").ap()
    MT = M // 128
    b = Bld(nc)
    import contextlib
    stack = contextlib.ExitStack()
    x = stack.enter_context(nc.sbuf_tensor("x", [128, D], F32))
    u = stack.enter_context(nc.sbuf_tensor("u", [128, D], F32))
    with stack:
        outtoks = []
        guard = []
        for mt in range(MT):
            sl = slice(mt * 128, (mt + 1) * 128)
            d1 = b.dma("sync", x[:], X[sl, :], deps=guard)
            d2 = b.dma("sync", u[:], U[sl, :], deps=guard)
            m1 = b.op("vector", lambda e: e.scalar_tensor_tensor(x[:], u[:], 0.5, x[:], ALU.mult, ALU.add),
                      deps=[d1, d2])
            s1 = b.op("scalar", lambda e: e.activation(x[:], x[:], AF.Silu), deps=[m1])
            od = b.dma("gpsimd", O[sl, :], x[:], deps=[s1])
            outtoks.append(od)
            guard = [s1, od]
        b.finish(outtoks[-2:])
    return b.finalize()

def np_g1silu(X, U):
    t = X + 0.5 * U
    return (t / (1.0 + np.exp(-t))).astype(np.float32)

def build_masksum(D, R):
    nc = bass.Bass("TRN2", target_bir_lowering=False)
    Cc = nc.dram_tensor("Cc", [D, 3 * R], F32, kind="ExternalInput").ap()
    Mm = nc.dram_tensor("Mm", [D, 3 * R], F32, kind="ExternalInput").ap()
    O = nc.dram_tensor("O", [D, R], F32, kind="ExternalOutput").ap()
    DT = D // 128
    b = Bld(nc)
    import contextlib
    stack = contextlib.ExitStack()
    c = stack.enter_context(nc.sbuf_tensor("c", [128, 3 * R], F32))
    m = stack.enter_context(nc.sbuf_tensor("m", [128, 3 * R], F32))
    with stack:
        outtoks = []
        guard = []
        for dt in range(DT):
            sl = slice(dt * 128, (dt + 1) * 128)
            d1 = b.dma("sync", c[:], Cc[sl, :], deps=guard)
            d2 = b.dma("sync", m[:], Mm[sl, :], deps=guard)
            m1 = b.op("vector", lambda e: e.tensor_tensor(c[:], c[:], m[:], ALU.mult), deps=[d1, d2])
            m2 = b.op("vector", lambda e: e.tensor_tensor(c[:, 0:R], c[:, 0:R], c[:, R:2 * R], ALU.add), deps=[m1])
            m3 = b.op("vector", lambda e: e.tensor_tensor(c[:, 0:R], c[:, 0:R], c[:, 2 * R:3 * R], ALU.add), deps=[m2])
            od = b.dma("gpsimd", O[sl, :], c[:, 0:R], deps=[m3])
            outtoks.append(od)
            guard = [m3, od]
        b.finish(outtoks[-2:])
    return b.finalize()

def np_masksum(C, M, R):
    t = C * M
    return (t[:, 0:R] + t[:, R:2 * R] + t[:, 2 * R:3 * R]).astype(np.float32)

CSHIFT = 16.0
LN1E10 = math.log(1e-10)

def build_csim():
    nc = bass.Bass("TRN2", target_bir_lowering=False)
    QT = nc.dram_tensor("QT", [512, 4096], F32, kind="ExternalInput").ap()
    CKT = nc.dram_tensor("CKT", [128, 257], F32, kind="ExternalInput").ap()
    CV = nc.dram_tensor("CV", [384, 132], F32, kind="ExternalInput").ap()
    CM = nc.dram_tensor("CM", [4096, 257], F32, kind="ExternalInput").ap()
    IDE = nc.dram_tensor("IDE", [128, 128], BF16, kind="ExternalInput").ap()
    PVC = nc.dram_tensor("PVC", [4096, 516], F32, kind="ExternalOutput").ap()
    BM = nc.dram_tensor("BM", [4096, 256], F32, kind="ExternalOutput").ap()
    b = Bld(nc)
    import contextlib
    stack = contextlib.ExitStack()
    qt32 = stack.enter_context(nc.sbuf_tensor("qt32", [128, 4 * 4096], F32))
    qtb = stack.enter_context(nc.sbuf_tensor("qtb", [128, 4 * 4096], BF16))
    ckb = stack.enter_context(nc.sbuf_tensor("ckb", [128, 257], BF16))
    ck32 = stack.enter_context(nc.sbuf_tensor("ck32", [128, 257], F32))
    cvb = stack.enter_context(nc.sbuf_tensor("cvb", [128, 3 * 132], BF16))
    cv32 = stack.enter_context(nc.sbuf_tensor("cv32", [128, 3 * 132], F32))
    ident = stack.enter_context(nc.sbuf_tensor("ident", [128, 128], BF16))
    cmt = stack.enter_context(nc.sbuf_tensor("cmt", [128, 257], F32))
    csm = stack.enter_context(nc.sbuf_tensor("csm", [128, 257], F32))
    imp = stack.enter_context(nc.sbuf_tensor("imp", [128, 256], F32))
    escr = stack.enter_context(nc.sbuf_tensor("escr", [128, 256], F32))
    pr = stack.enter_context(nc.sbuf_tensor("pr", [128, 257], BF16))
    prT = stack.enter_context(nc.sbuf_tensor("prT", [128, 384], BF16))
    col = stack.enter_context(nc.sbuf_tensor("col", [128, 16], F32))
    cns = stack.enter_context(nc.sbuf_tensor("cns", [128, 6], F32))
    pvev = stack.enter_context(nc.sbuf_tensor("pvev", [128, 516], F32))
    bmv = stack.enter_context(nc.sbuf_tensor("bmv", [128, 256], F32))
    ps_cs = stack.enter_context(nc.psum_tensor("ps_cs", [128, 512], F32))
    ps_t = stack.enter_context(nc.psum_tensor("ps_t", [128, 1024], BF16))
    ps_pv = stack.enter_context(nc.psum_tensor("ps_pv", [128, 512], F32))
    grd = stack.enter_context(nc.sbuf_tensor("grd", [1, 4], F32))
    b.set_guard(grd)
    with stack:
        c0 = b.op("vector", lambda e: e.memset(cns[:, 0:1], 0.25 * SCALE))
        c1 = b.op("vector", lambda e: e.memset(cns[:, 1:2], -CSHIFT), deps=[c0])
        c2 = b.op("vector", lambda e: e.memset(cns[:, 2:3], -1000.0), deps=[c1])
        c3 = b.op("vector", lambda e: e.memset(cns[:, 3:4], 1.0), deps=[c2])
        c4 = b.op("vector", lambda e: e.memset(cns[:, 4:5], SCALE), deps=[c3])
        ccst = c4
        d1 = b.dma("sync", qt32[:].rearrange("p (h q) -> p h q", h=4),
                   QT.rearrange("(h p) q -> p h q", p=128))
        cq = b.op("vector", lambda e: e.tensor_copy(qtb[:], qt32[:]), deps=[d1, ccst])
        d2 = b.dma("sync", ck32[:], CKT)
        cck = b.op("vector", lambda e: e.tensor_copy(ckb[:], ck32[:]), deps=[d2, cq])
        d3 = b.dma("sync", cv32[:].rearrange("p (r c) -> p r c", r=3),
                   CV.rearrange("(r p) c -> p r c", p=128))
        ccv = b.op("vector", lambda e: e.tensor_copy(cvb[:], cv32[:]), deps=[d3, cck])
        d4 = b.dma("sync", ident[:], IDE)
        outtoks = []
        t_pvev = None
        t_bm = None
        t_imp_rd = None
        t_bmo_prev = None
        t_lz_prev = None
        for qt in range(32):
            qsl = slice(qt * 128, (qt + 1) * 128)
            dm = b.dma("sync", cmt[:], CM[qsl, :], deps=[t_imp_rd])
            t_imp = None
            t_pv_dma = None
            t_ex_prev = None
            t_ma_prev = None
            t_pv3_prev = None
            for h in range(4):
                mm = b.op("tensor", lambda e, h=h, qt=qt: e.matmul(
                    ps_cs[:, 0:257], qtb[:, h * 4096 + qt * 128: h * 4096 + qt * 128 + 128],
                    ckb[:], start=True, stop=True),
                    deps=[cq, cck, (None if h else t_imp_rd), t_ma_prev])
                ma = b.op("vector", lambda e: e.tensor_tensor(csm[:], ps_cs[:, 0:257], cmt[:], ALU.add),
                          deps=[mm, dm, t_ex_prev, t_imp])
                if h == 0:
                    t_imp = b.op("vector", lambda e: e.tensor_copy(imp[:], csm[:, 0:256]), deps=[ma])
                else:
                    t_imp = b.op("vector", lambda e: e.tensor_tensor(imp[:], imp[:], csm[:, 0:256], ALU.add),
                                 deps=[ma, t_imp])
                ex = b.op("scalar", lambda e: e.activation(pr[:], csm[:], AF.Exp, bias=cns[:, 1:2],
                                                           scale=cns[:, 4:5]),
                          deps=[ma, t_pv3_prev])
                tchc = b.touch("scalar", deps=[ex])
                tt0 = b.op("tensor", lambda e: e.transpose(ps_t[:, 0:128], pr[:, 0:128], ident[:]),
                           deps=[ex, tchc, d4, t_pv_dma])
                tt1 = b.op("tensor", lambda e: e.transpose(ps_t[:, 128:256], pr[:, 128:256], ident[:]),
                           deps=[ex])
                tt2 = b.op("tensor", lambda e: e.transpose(ps_t[0:1, 256:384], pr[:, 256:257], ident[:]),
                           deps=[ex])
                cpt = b.op("vector", lambda e: e.tensor_copy(prT[:, 0:384], ps_t[:, 0:384]),
                           deps=[tt0, tt1, tt2])
                cpt = b.touch("vector", deps=[cpt])
                pv1 = b.op("tensor", lambda e: e.matmul(ps_pv[:, 0:129], prT[:, 0:128], cvb[:, 0:129],
                                                        start=True, stop=False), deps=[cpt, ccv, t_pvev])
                pv2 = b.op("tensor", lambda e: e.matmul(ps_pv[:, 0:129], prT[:, 128:256], cvb[:, 132:132 + 129],
                                                        start=False, stop=False), deps=[pv1])
                pv3 = b.op("tensor", lambda e: e.matmul(ps_pv[:, 0:129], prT[0:1, 256:384], cvb[0:1, 264:264 + 129],
                                                        start=False, stop=True), deps=[pv2])
                evv = b.op("scalar", lambda e, h=h: e.activation(pvev[:, h * 129:(h + 1) * 129],
                                                                 ps_pv[:, 0:129], AF.Copy),
                           deps=[pv3, t_pvev])
                t_pvev = evv
                t_pv_dma = evv
                t_ex_prev = ex
                t_ma_prev = ma
                t_pv3_prev = pv3
                t_imp = t_imp
            od1 = b.dma_out("gpsimd", PVC[qsl, :], pvev[:, 0:516], t_pvev, "scalar")
            outtoks.append(od1)
            t_pvev = od1
            rm = b.op("vector", lambda e: e.reduce_max(col[:, 0:1], imp[:], axis=mybir.AxisListType.X),
                      deps=[t_imp, t_lz_prev])
            nm = b.op("vector", lambda e: e.tensor_scalar(col[:, 1:2], col[:, 0:1], -0.25 * SCALE, 1000.0,
                                                          ALU.mult, ALU.min), deps=[rm])
            ez = b.op("scalar", lambda e: e.activation(escr[:], imp[:], AF.Exp, bias=col[:, 1:2],
                                                       scale=cns[:, 0:1], accum_out=col[:, 2:3]),
                      deps=[nm, t_imp, t_bmo_prev])
            sent = b.op("scalar", lambda e: e.activation(col[:, 3:4], col[:, 1:2], AF.Exp, bias=cns[:, 2:3]),
                        deps=[nm, ez])
            tchs = b.touch("scalar", deps=[sent])
            zt = b.op("vector", lambda e: e.tensor_tensor(col[:, 4:5], col[:, 2:3], col[:, 3:4], ALU.add),
                      deps=[ez, sent, tchs])
            lz = b.op("scalar", lambda e: e.activation(col[:, 5:6], col[:, 4:5], AF.Ln), deps=[zt])
            tchl = b.touch("scalar", deps=[lz])
            th = b.op("vector", lambda e: e.tensor_tensor(col[:, 6:7], col[:, 5:6], col[:, 1:2], ALU.subtract),
                      deps=[lz, tchl])
            th2 = b.op("vector", lambda e: e.tensor_scalar(col[:, 6:7], col[:, 6:7], LN1E10,
                                                           1.0 / (0.25 * SCALE),
                                                           ALU.add, ALU.mult), deps=[th])
            t8 = b.op("vector", lambda e: e.max(out=col[:, 8:16], in_=imp[:]), deps=[th2, t_imp])
            ge = b.op("vector", lambda e: e.tensor_scalar(escr[:], imp[:], col[:, 11:12], None, ALU.is_ge),
                      deps=[t8, ez])
            bmo = b.op("vector", lambda e: e.scalar_tensor_tensor(bmv[:], imp[:], col[:, 6:7], escr[:],
                                                                  ALU.is_gt, ALU.mult), deps=[ge, t_bm])
            od2 = b.dma_out("gpsimd", BM[qsl, :], bmv[:], bmo, "vector")
            outtoks.append(od2)
            t_bm = od2
            t_imp_rd = bmo
            t_bmo_prev = bmo
            t_lz_prev = lz
        b.finish(outtoks[-2:])
    return b.finalize()

def np_csim(QT, CKT, CV, CM):
    q = QT.astype(BF).astype(np.float32)
    ck = CKT.astype(BF).astype(np.float32)
    cv = CV.astype(BF).astype(np.float32)
    pvc = np.zeros((4096, 516), np.float32)
    imp = np.zeros((4096, 256), np.float32)
    for h in range(4):
        qh = q[h * 128:(h + 1) * 128, :]
        cs = (qh.T @ ck) + CM
        imp += cs[:, 0:256]
        pr = np.exp(SCALE * cs - CSHIFT).astype(BF).astype(np.float32)
        keys = np.concatenate([cv[0:256, 0:129], cv[256:257, 0:129]], axis=0)
        pvc[:, h * 129:(h + 1) * 129] = pr @ keys
    sc = 0.25 * SCALE
    rm = imp.max(axis=1, keepdims=True)
    nm = np.minimum(-sc * rm, 1000.0)
    e = np.exp(sc * imp + nm)
    zs = e.sum(axis=1, keepdims=True)
    sent = np.exp(nm - 1000.0)
    zt = zs + sent
    thr = ((np.log(zt) - nm) + LN1E10) / sc
    top8 = -np.sort(-imp, axis=1)[:, 0:8]
    t4 = top8[:, 3:4]
    bm = ((imp >= t4) & (imp > thr)).astype(np.float32)
    return pvc, bm

FSHIFT = 16.0

def build_fine():
    nc = bass.Bass("TRN2", target_bir_lowering=False)
    RQT = nc.dram_tensor("RQT", [512, 4096], F32, kind="ExternalInput").ap()
    RKT = nc.dram_tensor("RKT", [128, 4096], F32, kind="ExternalInput").ap()
    V = nc.dram_tensor("V", [4096, 128], F32, kind="ExternalInput").ap()
    BMX = nc.dram_tensor("BMX", [4096, 4096], BF16, kind="ExternalInput").ap()
    OWNT = nc.dram_tensor("OWNT", [4096, 4096], BF16, kind="ExternalInput").ap()
    WMT = nc.dram_tensor("WMT", [4096, 4096], BF16, kind="ExternalInput").ap()
    OF = nc.dram_tensor("OF", [512, 4096], F32, kind="ExternalOutput").ap()
    OW = nc.dram_tensor("OW", [512, 4096], F32, kind="ExternalOutput").ap()
    ZF = nc.dram_tensor("ZF", [4, 4096], F32, kind="ExternalOutput").ap()
    ZW = nc.dram_tensor("ZW", [4, 4096], F32, kind="ExternalOutput").ap()
    b = Bld(nc)
    import contextlib
    stack = contextlib.ExitStack()
    rq32 = stack.enter_context(nc.sbuf_tensor("rq32", [128, 4 * 4096], F32))
    rqb = stack.enter_context(nc.sbuf_tensor("rqb", [128, 4 * 4096], BF16))
    rk32 = stack.enter_context(nc.sbuf_tensor("rk32", [128, 4096], F32))
    rkb = stack.enter_context(nc.sbuf_tensor("rkb", [128, 4096], BF16))
    v32 = stack.enter_context(nc.sbuf_tensor("v32", [128, 32 * 128], F32))
    vb = stack.enter_context(nc.sbuf_tensor("vb", [128, 32 * 128], BF16))
    onec = stack.enter_context(nc.sbuf_tensor("onec", [128, 1], BF16))
    cns = stack.enter_context(nc.sbuf_tensor("cns", [128, 2], F32))
    mf = stack.enter_context(nc.sbuf_tensor("mf", [128, 32 * 512], BF16))
    wmc = stack.enter_context(nc.sbuf_tensor("wmc", [128, 5 * 512], BF16))
    eT = stack.enter_context(nc.sbuf_tensor("eT", [128, 512], BF16))
    pf = stack.enter_context(nc.sbuf_tensor("pf", [128, 512], BF16))
    pw = stack.enter_context(nc.sbuf_tensor("pw", [128, 512], BF16))
    msk32 = stack.enter_context(nc.sbuf_tensor("msk32", [128, 512], BF16))
    ev = stack.enter_context(nc.sbuf_tensor("ev", [128, 512], F32))
    evz = stack.enter_context(nc.sbuf_tensor("evz", [1, 1024], F32))
    ps_s = [stack.enter_context(nc.psum_tensor(f"ps_s{i}", [128, 512], F32)) for i in range(2)]
    ps_of = stack.enter_context(nc.psum_tensor("ps_of", [128, 512], F32))
    ps_ow = stack.enter_context(nc.psum_tensor("ps_ow", [128, 512], F32))
    ps_zf = stack.enter_context(nc.psum_tensor("ps_zf", [128, 512], F32))
    ps_zw = stack.enter_context(nc.psum_tensor("ps_zw", [128, 512], F32))
    grd = stack.enter_context(nc.sbuf_tensor("grd", [1, 4], F32))
    b.set_guard(grd)
    with stack:
        cc = b.op("vector", lambda e: e.memset(cns[:, 0:1], -FSHIFT))
        cs2 = b.op("vector", lambda e: e.memset(cns[:, 1:2], SCALE), deps=[cc])
        co = b.op("vector", lambda e: e.memset(onec[:], 1.0), deps=[cs2])
        d1 = b.dma("sync", rq32[:].rearrange("p (h q) -> p h q", h=4),
                   RQT.rearrange("(h p) q -> p h q", p=128))
        c1 = b.op("vector", lambda e: e.tensor_copy(rqb[:], rq32[:]), deps=[d1, co])
        d2 = b.dma("sync", rk32[:], RKT)
        c2 = b.op("vector", lambda e: e.tensor_copy(rkb[:], rk32[:]), deps=[d2, c1])
        d3 = b.dma("sync", v32[:].rearrange("p (t c) -> p t c", t=32),
                   V.rearrange("(t p) c -> p t c", p=128))
        c3 = b.op("vector", lambda e: e.tensor_copy(vb[:], v32[:]), deps=[d3, c2])
        outtoks = []
        t_mask_rd = None
        t_ev = {}
        t_qk = {}
        t_exp = None
        t_pf = None
        t_pvw = None
        t_zw = None
        t_eT_readers = None
        for qg in range(8):
            q0 = qg * 512
            KC = 4 * (qg + 1)
            mtoks = []
            for kc in range(KC):
                dmb = b.dma("sync", mf[:, kc * 512:(kc + 1) * 512].bitcast(BF16) if False else mf[:, kc * 512:(kc + 1) * 512],
                            BMX[kc * 128:(kc + 1) * 128, q0:q0 + 512], deps=[t_mask_rd])
                dmo = b.dma("scalar", msk32[:], OWNT[kc * 128:(kc + 1) * 128, q0:q0 + 512], deps=[t_mask_rd])
                mt = b.op("vector", lambda e, kc=kc: e.tensor_tensor(
                    mf[:, kc * 512:(kc + 1) * 512], mf[:, kc * 512:(kc + 1) * 512], msk32[:], ALU.add),
                    deps=[dmb, dmo])
                t_mask_rd = mt
                mtoks.append(mt)
            wtoks = []
            wkcs = [kc for kc in range(max(0, 4 * qg - 1), 4 * qg + 4)]
            for wi, kc in enumerate(wkcs):
                dw = b.dma("scalar", wmc[:, wi * 512:(wi + 1) * 512],
                           WMT[kc * 128:(kc + 1) * 128, q0:q0 + 512], deps=[t_mask_rd])
                wtoks.append(dw)
            for h in range(4):
                t_pvf = None
                t_zf = None
                for kc in range(KC):
                    pp = ps_s[kc % 2]
                    qk = b.op("tensor", lambda e, kc=kc, h=h, q0=q0, pp=pp: e.matmul(
                        pp[:, 0:512], rkb[:, kc * 128:(kc + 1) * 128],
                        rqb[:, h * 4096 + q0: h * 4096 + q0 + 512], start=True, stop=True),
                        deps=[c1, c2, t_qk.get(kc % 2)])
                    ex = b.op("scalar", lambda e, pp=pp: e.activation(eT[:], pp[:, 0:512], AF.Exp,
                                                                      bias=cns[:, 0:1],
                                                                      scale=cns[:, 1:2]),
                              deps=[qk, t_pf, t_exp, t_eT_readers])
                    ex = b.touch("scalar", deps=[ex])
                    t_exp = ex
                    t_qk[kc % 2] = ex
                    pft = b.op("vector", lambda e, kc=kc: e.tensor_tensor(
                        pf[:], eT[:], mf[:, kc * 512:(kc + 1) * 512], ALU.mult),
                        deps=[ex, mtoks[kc] if kc < len(mtoks) else None, t_pf])
                    pft = b.touch("vector", deps=[pft])
                    t_eT_readers = pft
                    pvf = b.op("tensor", lambda e, kc=kc, st=(kc == 0): e.matmul(
                        ps_of[:, 0:512], vb[:, kc * 128:(kc + 1) * 128], pf[:],
                        start=st, stop=(kc == KC - 1)), deps=[pft, c3, (t_ev.get("of") if kc == 0 else None)])
                    zf = b.op("tensor", lambda e, st=(kc == 0): e.matmul(
                        ps_zf[0:1, 0:512], onec[:], pf[:],
                        start=st, stop=(kc == KC - 1)), deps=[pft, co, (t_ev.get("zf") if kc == 0 else None)])
                    t_pvf = pvf
                    t_zf = zf
                    t_pf = pvf
                    t_pf = zf
                    if kc in wkcs:
                        wi = wkcs.index(kc)
                        pwt = b.op("vector", lambda e, wi=wi: e.tensor_tensor(
                            pw[:], eT[:], wmc[:, wi * 512:(wi + 1) * 512], ALU.mult),
                            deps=[ex, wtoks[wi], t_pvw, t_zw])
                        pwt = b.touch("vector", deps=[pwt])
                        t_eT_readers = pwt
                        pvw = b.op("tensor", lambda e, kc=kc, st=(wi == 0): e.matmul(
                            ps_ow[:, 0:512], vb[:, kc * 128:(kc + 1) * 128], pw[:],
                            start=st, stop=(wi == len(wkcs) - 1)),
                            deps=[pwt, c3, (t_ev.get("ow") if wi == 0 else None)])
                        zw = b.op("tensor", lambda e, st=(wi == 0): e.matmul(
                            ps_zw[0:1, 0:512], onec[:], pw[:],
                            start=st, stop=(wi == len(wkcs) - 1)),
                            deps=[pwt, (t_ev.get("zw") if wi == 0 else None)])
                        t_pvw = pvw
                        t_zw = zw
                        t_mask_rd = pwt
                e1 = b.op("scalar", lambda e: e.activation(ev[:], ps_of[:, 0:512], AF.Copy),
                          deps=[t_pvf, t_ev.get("ofd")])
                o1 = b.dma_out("gpsimd", OF[h * 128:(h + 1) * 128, q0:q0 + 512], ev[:], e1, "scalar")
                t_ev["of"] = e1
                t_ev["ofd"] = o1
                e2 = b.op("scalar", lambda e: e.activation(ev[:], ps_ow[:, 0:512], AF.Copy),
                          deps=[t_pvw, o1])
                o2 = b.dma_out("gpsimd", OW[h * 128:(h + 1) * 128, q0:q0 + 512], ev[:], e2, "scalar")
                t_ev["ow"] = e2
                t_ev["ofd"] = o2
                e3 = b.op("scalar", lambda e: e.activation(evz[0:1, 0:512], ps_zf[0:1, 0:512], AF.Copy),
                          deps=[t_zf, t_ev.get("zfd")])
                o3 = b.dma_out("gpsimd", ZF[h:h + 1, q0:q0 + 512], evz[0:1, 0:512], e3, "scalar")
                t_ev["zf"] = e3
                t_ev["zfd"] = o3
                e4 = b.op("scalar", lambda e: e.activation(evz[0:1, 512:1024], ps_zw[0:1, 0:512], AF.Copy),
                          deps=[t_zw, t_ev.get("zwd")])
                o4 = b.dma_out("gpsimd", ZW[h:h + 1, q0:q0 + 512], evz[0:1, 512:1024], e4, "scalar")
                t_ev["zw"] = e4
                t_ev["zwd"] = o4
                outtoks += [o1, o2, o3, o4]
                t_mask_rd = pft
        b.finish(outtoks[-4:])
    return b.finalize()

def np_fine(RQT, RKT, V, BMX, OWNT, WMT):
    rq = RQT.astype(BF).astype(np.float32)
    rk = RKT.astype(BF).astype(np.float32)
    v = V.astype(BF).astype(np.float32)
    bmx = np.asarray(BMX, np.float32)
    ownt = np.asarray(OWNT, np.float32)
    wmt = np.asarray(WMT, np.float32)
    OFo = np.zeros((512, 4096), np.float32)
    OWo = np.zeros((512, 4096), np.float32)
    ZFo = np.zeros((4, 4096), np.float32)
    ZWo = np.zeros((4, 4096), np.float32)
    for qg in range(8):
        q0 = qg * 512
        KC = 4 * (qg + 1)
        kmax = KC * 128
        mfull = bmx[0:kmax, q0:q0 + 512] + ownt[0:kmax, q0:q0 + 512]
        wlo = max(0, 4 * qg - 1) * 128
        whi = (4 * qg + 4) * 128
        for h in range(4):
            sT = rk[:, 0:kmax].T @ rq[h * 128:(h + 1) * 128, q0:q0 + 512]
            eT = np.exp(sT * SCALE - FSHIFT).astype(BF).astype(np.float32)
            pfv = (eT * mfull).astype(BF).astype(np.float32)
            OFo[h * 128:(h + 1) * 128, q0:q0 + 512] = v[0:kmax].T @ pfv
            ZFo[h, q0:q0 + 512] = pfv.sum(axis=0)
            ww = wmt[wlo:whi, q0:q0 + 512]
            pwv = (eT[wlo:whi] * ww).astype(BF).astype(np.float32)
            OWo[h * 128:(h + 1) * 128, q0:q0 + 512] = v[wlo:whi].T @ pwv
            ZWo[h, q0:q0 + 512] = pwv.sum(axis=0)
    return OFo, OWo, ZFo, ZWo

L1SHIFT = 20.0

def build_l1attn():
    nc = bass.Bass("TRN2", target_bir_lowering=False)
    RQT = nc.dram_tensor("RQT", [512, 4096], F32, kind="ExternalInput").ap()
    RKT = nc.dram_tensor("RKT", [128, 4096], F32, kind="ExternalInput").ap()
    V = nc.dram_tensor("V", [4096, 128], F32, kind="ExternalInput").ap()
    OT = nc.dram_tensor("OT", [512, 4096], F32, kind="ExternalOutput").ap()
    Z = nc.dram_tensor("Z", [4, 4096], F32, kind="ExternalOutput").ap()
    b = Bld(nc)
    import contextlib
    stack = contextlib.ExitStack()
    rq32 = stack.enter_context(nc.sbuf_tensor("rq32", [128, 4 * 4096], F32))
    rqb = stack.enter_context(nc.sbuf_tensor("rqb", [128, 4 * 4096], BF16))
    rk32 = stack.enter_context(nc.sbuf_tensor("rk32", [128, 4096], F32))
    rkb = stack.enter_context(nc.sbuf_tensor("rkb", [128, 4096], BF16))
    v32 = stack.enter_context(nc.sbuf_tensor("v32", [128, 32 * 128], F32))
    vb = stack.enter_context(nc.sbuf_tensor("vb", [128, 32 * 128], BF16))
    onec = stack.enter_context(nc.sbuf_tensor("onec", [128, 1], BF16))
    cns = stack.enter_context(nc.sbuf_tensor("cns", [128, 2], F32))
    eT = stack.enter_context(nc.sbuf_tensor("eT", [128, 512], BF16))
    ev = stack.enter_context(nc.sbuf_tensor("ev", [128, 512], F32))
    evz = stack.enter_context(nc.sbuf_tensor("evz", [1, 512], F32))
    ps_s = [stack.enter_context(nc.psum_tensor(f"ps_s{i}", [128, 512], F32)) for i in range(2)]
    ps_o = stack.enter_context(nc.psum_tensor("ps_o", [128, 512], F32))
    ps_z = stack.enter_context(nc.psum_tensor("ps_z", [128, 512], F32))
    grd = stack.enter_context(nc.sbuf_tensor("grd", [1, 4], F32))
    b.set_guard(grd)
    with stack:
        cc = b.op("vector", lambda e: e.memset(cns[:, 0:1], -L1SHIFT))
        cs2 = b.op("vector", lambda e: e.memset(cns[:, 1:2], SCALE), deps=[cc])
        co = b.op("vector", lambda e: e.memset(onec[:], 1.0), deps=[cs2])
        d1 = b.dma("sync", rq32[:].rearrange("p (h q) -> p h q", h=4),
                   RQT.rearrange("(h p) q -> p h q", p=128))
        c1 = b.op("vector", lambda e: e.tensor_copy(rqb[:], rq32[:]), deps=[d1, co])
        d2 = b.dma("sync", rk32[:], RKT)
        c2 = b.op("vector", lambda e: e.tensor_copy(rkb[:], rk32[:]), deps=[d2, c1])
        d3 = b.dma("sync", v32[:].rearrange("p (t c) -> p t c", t=32),
                   V.rearrange("(t p) c -> p t c", p=128))
        c3 = b.op("vector", lambda e: e.tensor_copy(vb[:], v32[:]), deps=[d3, c2])
        outtoks = []
        t_ev = {}
        t_exp = None
        t_qk = {}
        t_pv = None
        t_z = None
        for qg in range(8):
            q0 = qg * 512
            for h in range(4):
                for kc in range(32):
                    pp = ps_s[kc % 2]
                    qk = b.op("tensor", lambda e, kc=kc, h=h, q0=q0, pp=pp: e.matmul(
                        pp[:, 0:512], rkb[:, kc * 128:(kc + 1) * 128],
                        rqb[:, h * 4096 + q0: h * 4096 + q0 + 512], start=True, stop=True),
                        deps=[c1, c2, t_qk.get(kc % 2)])
                    ex = b.op("scalar", lambda e, pp=pp: e.activation(eT[:], pp[:, 0:512], AF.Exp,
                                                                      bias=cns[:, 0:1],
                                                                      scale=cns[:, 1:2]),
                              deps=[qk, t_pv, t_z, t_exp])
                    tch = b.touch("scalar", deps=[ex])
                    t_exp = tch
                    t_qk[kc % 2] = ex
                    pv = b.op("tensor", lambda e, kc=kc, st=(kc == 0): e.matmul(
                        ps_o[:, 0:512], vb[:, kc * 128:(kc + 1) * 128], eT[:],
                        start=st, stop=(kc == 31)), deps=[ex, tch, c3, (t_ev.get("o") if kc == 0 else None)])
                    z = b.op("tensor", lambda e, st=(kc == 0): e.matmul(
                        ps_z[0:1, 0:512], onec[:], eT[:],
                        start=st, stop=(kc == 31)), deps=[ex, tch, (t_ev.get("z") if kc == 0 else None)])
                    t_pv = pv
                    t_z = z
                e1 = b.op("scalar", lambda e: e.activation(ev[:], ps_o[:, 0:512], AF.Copy),
                          deps=[t_pv, t_ev.get("od")])
                o1 = b.dma_out("gpsimd", OT[h * 128:(h + 1) * 128, q0:q0 + 512], ev[:], e1, "scalar")
                t_ev["o"] = e1
                t_ev["od"] = o1
                e2 = b.op("scalar", lambda e: e.activation(evz[0:1, 0:512], ps_z[0:1, 0:512], AF.Copy),
                          deps=[t_z, t_ev.get("zd")])
                o2 = b.dma_out("gpsimd", Z[h:h + 1, q0:q0 + 512], evz[0:1, 0:512], e2, "scalar")
                t_ev["z"] = e2
                t_ev["zd"] = o2
                outtoks += [o1, o2]
        b.finish(outtoks[-2:])
    return b.finalize()

def np_l1attn(RQT, RKT, V):
    rq = RQT.astype(BF).astype(np.float32)
    rk = RKT.astype(BF).astype(np.float32)
    v = V.astype(BF).astype(np.float32)
    OTo = np.zeros((512, 4096), np.float32)
    Zo = np.zeros((4, 4096), np.float32)
    for h in range(4):
        sT = rk.T @ rq[h * 128:(h + 1) * 128, :]
        eT = np.exp(sT * SCALE - L1SHIFT).astype(BF).astype(np.float32)
        OTo[h * 128:(h + 1) * 128, :] = v.T @ eT
        Zo[h, :] = eT.sum(axis=0)
    return OTo, Zo

def build_combine0():
    nc = bass.Bass("TRN2", target_bir_lowering=False)
    PVC = nc.dram_tensor("PVC", [4096, 516], F32, kind="ExternalInput").ap()
    OFN = nc.dram_tensor("OFN", [4096, 512], F32, kind="ExternalInput").ap()
    OWN = nc.dram_tensor("OWN", [4096, 512], F32, kind="ExternalInput").ap()
    ZF = nc.dram_tensor("ZFn", [4096, 4], F32, kind="ExternalInput").ap()
    ZW = nc.dram_tensor("ZWn", [4096, 4], F32, kind="ExternalInput").ap()
    G = nc.dram_tensor("G", [4096, 12], F32, kind="ExternalInput").ap()
    O = nc.dram_tensor("O", [4096, 512], F32, kind="ExternalOutput").ap()
    b = Bld(nc)
    import contextlib
    stack = contextlib.ExitStack()
    pvc = stack.enter_context(nc.sbuf_tensor("pvc", [128, 516], F32))
    ofn = stack.enter_context(nc.sbuf_tensor("ofn", [128, 512], F32))
    own = stack.enter_context(nc.sbuf_tensor("own", [128, 512], F32))
    zf = stack.enter_context(nc.sbuf_tensor("zf", [128, 4], F32))
    zw = stack.enter_context(nc.sbuf_tensor("zw", [128, 4], F32))
    g = stack.enter_context(nc.sbuf_tensor("g", [128, 12], F32))
    col = stack.enter_context(nc.sbuf_tensor("col", [128, 8], F32))
    o = stack.enter_context(nc.sbuf_tensor("o", [128, 512], F32))
    grd = stack.enter_context(nc.sbuf_tensor("grd", [1, 4], F32))
    b.set_guard(grd)
    with stack:
        outtoks = []
        guard = []
        for qt in range(32):
            sl = slice(qt * 128, (qt + 1) * 128)
            d1 = b.dma("sync", pvc[:], PVC[sl, :], deps=guard)
            d2 = b.dma("sync", ofn[:], OFN[sl, :], deps=guard)
            d3 = b.dma("sync", own[:], OWN[sl, :], deps=guard)
            d4 = b.dma("sync", zf[:], ZF[sl, :], deps=guard)
            d5 = b.dma("sync", zw[:], ZW[sl, :], deps=guard)
            d6 = b.dma("sync", g[:], G[sl, :], deps=guard)
            tprev = None
            for h in range(4):
                r1 = b.op("vector", lambda e, h=h: e.reciprocal(col[:, 0:1], pvc[:, h * 129 + 128:h * 129 + 129]),
                          deps=[d1, tprev])
                r2 = b.op("vector", lambda e, h=h: e.reciprocal(col[:, 1:2], zf[:, h:h + 1]), deps=[d4, r1])
                r3 = b.op("vector", lambda e, h=h: e.reciprocal(col[:, 2:3], zw[:, h:h + 1]), deps=[d5, r2])
                a1 = b.op("vector", lambda e, h=h: e.tensor_tensor(col[:, 3:4], col[:, 0:1], g[:, 3 * h:3 * h + 1],
                                                                   ALU.mult), deps=[r1, d6])
                a2 = b.op("vector", lambda e, h=h: e.tensor_tensor(col[:, 4:5], col[:, 1:2], g[:, 3 * h + 1:3 * h + 2],
                                                                   ALU.mult), deps=[r2, d6])
                a3 = b.op("vector", lambda e, h=h: e.tensor_tensor(col[:, 5:6], col[:, 2:3], g[:, 3 * h + 2:3 * h + 3],
                                                                   ALU.mult), deps=[r3, d6])
                s1 = b.op("vector", lambda e, h=h: e.tensor_scalar(o[:, h * 128:(h + 1) * 128],
                                                                   pvc[:, h * 129:h * 129 + 128],
                                                                   col[:, 3:4], None, ALU.mult), deps=[a1])
                s2 = b.op("vector", lambda e, h=h: e.scalar_tensor_tensor(o[:, h * 128:(h + 1) * 128],
                                                                          ofn[:, h * 128:(h + 1) * 128],
                                                                          col[:, 4:5],
                                                                          o[:, h * 128:(h + 1) * 128],
                                                                          ALU.mult, ALU.add), deps=[s1, a2, d2])
                s3 = b.op("vector", lambda e, h=h: e.scalar_tensor_tensor(o[:, h * 128:(h + 1) * 128],
                                                                          own[:, h * 128:(h + 1) * 128],
                                                                          col[:, 5:6],
                                                                          o[:, h * 128:(h + 1) * 128],
                                                                          ALU.mult, ALU.add), deps=[s2, a3, d3])
                tprev = s3
            od = b.dma_out("gpsimd", O[sl, :], o[:], tprev, "vector")
            outtoks.append(od)
            guard = [tprev, od]
        b.finish(outtoks[-2:])
    return b.finalize()

def np_combine0(PVC, OFN, OWN, ZF, ZW, G):
    O = np.zeros((4096, 512), np.float32)
    for h in range(4):
        zc = PVC[:, h * 129 + 128:h * 129 + 129]
        ac = G[:, 3 * h:3 * h + 1] / zc
        af = G[:, 3 * h + 1:3 * h + 2] / ZF[:, h:h + 1]
        aw = G[:, 3 * h + 2:3 * h + 3] / ZW[:, h:h + 1]
        O[:, h * 128:(h + 1) * 128] = (PVC[:, h * 129:h * 129 + 128] * ac
                                       + OFN[:, h * 128:(h + 1) * 128] * af
                                       + OWN[:, h * 128:(h + 1) * 128] * aw)
    return O

def build_x2comb(M):
    nc = bass.Bass("TRN2", target_bir_lowering=False)
    X1 = nc.dram_tensor("X1", [M, 4096], F32, kind="ExternalInput").ap()
    O1 = nc.dram_tensor("O1", [M, 4096], F32, kind="ExternalInput").ap()
    Z1 = nc.dram_tensor("Z1", [M, 32], F32, kind="ExternalInput").ap()
    G1 = nc.dram_tensor("G1", [M, 4096], F32, kind="ExternalInput").ap()
    O = nc.dram_tensor("O", [M, 4096], F32, kind="ExternalOutput").ap()
    MT = M // 128
    b = Bld(nc)
    import contextlib
    stack = contextlib.ExitStack()
    x1 = stack.enter_context(nc.sbuf_tensor("x1", [128, 4096], F32))
    o1 = stack.enter_context(nc.sbuf_tensor("o1", [128, 4096], F32))
    z1 = stack.enter_context(nc.sbuf_tensor("z1", [128, 32], F32))
    rz = stack.enter_context(nc.sbuf_tensor("rz", [128, 32], F32))
    g1 = stack.enter_context(nc.sbuf_tensor("g1", [128, 4096], F32))
    grd = stack.enter_context(nc.sbuf_tensor("grd", [1, 4], F32))
    b.set_guard(grd)
    with stack:
        outtoks = []
        guard = []
        for mt in range(MT):
            sl = slice(mt * 128, (mt + 1) * 128)
            d1 = b.dma("sync", x1[:], X1[sl, :], deps=guard)
            d2 = b.dma("sync", o1[:], O1[sl, :], deps=guard)
            d3 = b.dma("sync", z1[:], Z1[sl, :], deps=guard)
            d4 = b.dma("sync", g1[:], G1[sl, :], deps=guard)
            r = b.op("vector", lambda e: e.reciprocal(rz[:], z1[:]), deps=[d3])
            tprev = r
            for h in range(32):
                t1 = b.op("vector", lambda e, h=h: e.tensor_scalar(o1[:, h * 128:(h + 1) * 128],
                                                                   o1[:, h * 128:(h + 1) * 128],
                                                                   rz[:, h:h + 1], None, ALU.mult),
                          deps=[d2, r, tprev])
                tprev = t1
            m1 = b.op("vector", lambda e: e.tensor_tensor(o1[:], o1[:], g1[:], ALU.mult), deps=[tprev, d4])
            m2 = b.op("vector", lambda e: e.tensor_tensor(x1[:], x1[:], o1[:], ALU.add), deps=[m1, d1])
            od = b.dma_out("gpsimd", O[sl, :], x1[:], m2, "vector")
            outtoks.append(od)
            guard = [m2, od]
        b.finish(outtoks[-2:])
    return b.finalize()

def np_x2comb(X1, O1, Z1, G1):
    o = O1.copy()
    for h in range(32):
        o[:, h * 128:(h + 1) * 128] /= Z1[:, h:h + 1]
    return (X1 + o * G1).astype(np.float32)


SIM = {"on": False, "check": False}

def run_stage(key, builder, in_maps, out_names, np_fn=None, np_args=None):
    if SIM["on"]:
        outs = []
        for c in range(NCORES):
            r = np_fn(*[in_maps[c][k] for k in np_args])
            if not isinstance(r, tuple):
                r = (r,)
            outs.append({n: v for n, v in zip(out_names, r)})
        return outs
    nc = get_prog(key, builder)
    res = runp(nc, in_maps)
    if SIM["check"]:
        import time
        for c in (0, NCORES - 1):
            r = np_fn(*[in_maps[c][k] for k in np_args])
            if not isinstance(r, tuple):
                r = (r,)
            for n, v in zip(out_names, r):
                got = np.asarray(res[c][n], np.float64)
                wv = np.asarray(v, np.float64)
                den = max(1e-30, float(np.sqrt(np.mean(wv ** 2))))
                err = float(np.sqrt(np.mean((got - wv) ** 2))) / den
                print(f"CHECK {key} core{c} {n}: relrms={err:.3e}", flush=True)
    return res


def _rope_tables(D2):
    t = (np.arange(8192) % 4096).astype(np.float64)
    inv = 1.0 / 10000.0 ** (np.arange(0, 128, 2, dtype=np.float64) / 128.0)
    ang = t[:, None] * inv[None, :]
    return np.cos(ang).astype(np.float32), np.sin(ang).astype(np.float32)


def _build_T():
    n = 4096
    t = np.arange(n).astype(np.float64)
    T = np.zeros((n, n), np.float64)
    T[:, 0] = 1.0 / math.sqrt(n)
    f = np.arange(1, n // 2)
    ang = 2.0 * np.pi * np.outer(t, f) / n
    T[:, 1:n - 1:2] = math.sqrt(2.0 / n) * np.cos(ang)
    T[:, 2:n - 1 + 1:2] = math.sqrt(2.0 / n) * np.sin(ang)
    T[:, n - 1] = ((-1.0) ** t) / math.sqrt(n)
    return T.astype(np.float32)


def kernel(**inp):
    X0 = np.asarray(inp["L0_attn_hidden_states"], np.float32).reshape(8192, 4096)
    H, KVH, DH = 32, 8, 128

    def shard_rows(a, n=NCORES):
        a = np.ascontiguousarray(a, np.float32)
        return [a.reshape(n, a.shape[0] // n, *a.shape[1:])[c] for c in range(n)]

    def cat_rows(outs, name):
        return np.concatenate([o[name] for o in outs], axis=0)

    ones_row = np.ones((1, 1024), np.float32)

    def mmT_launch(tag, A_list, B_list, K, M, N, act):
        Kpad = cdiv(K, 128) * 128
        ims = [{"A": padk(A_list[c], Kpad), "B": padk(B_list[c] if isinstance(B_list, list) else B_list, Kpad)}
               for c in range(NCORES)]
        outs = run_stage(("mmT", Kpad, M, N, act), lambda: build_mmT(Kpad, M, N, act), ims,
                         ["C"], lambda A, B, a=act: np_mmT(A, B, a), ["A", "B"])
        return outs

    outs = run_stage(("ln", 1024, 4096), lambda: build_ln(1024, 4096),
                     [{"X": s} for s in shard_rows(X0)], ["Y"], np_ln, ["X"])
    XN0 = cat_rows(outs, "Y")

    XN0T = np.ascontiguousarray(XN0.T)
    w_qkv = np.asarray(inp["L0_attn_w_qkv"], np.float32)
    A_l = [np.concatenate([XN0T[:, c * 1024:(c + 1) * 1024], np.zeros((1, 1024), np.float32)], axis=0)
           for c in range(NCORES)]
    Bq = np.concatenate([w_qkv.T, np.zeros((1, 6144), np.float32)], axis=0)
    outs = mmT_launch("qkv", A_l, Bq, 4097, 1024, 6144, "none")
    QKV = cat_rows(outs, "C")
    q0, k0, v0 = QKV[:, 0:4096], QKV[:, 4096:5120], QKV[:, 5120:6144]

    w_gate = np.asarray(inp["L0_attn_w_gate"], np.float32)
    b_gate = np.asarray(inp["L0_attn_b_gate"], np.float32)
    A_l = [np.concatenate([XN0T[:, c * 1024:(c + 1) * 1024], ones_row], axis=0) for c in range(NCORES)]
    Bg = np.concatenate([w_gate.T, b_gate[None, :]], axis=0)
    outs = mmT_launch("gates", A_l, Bg, 4097, 1024, 96, "sigmoid")
    GATES = cat_rows(outs, "C")

    cosT, sinT = _rope_tables(64)
    cq = np.tile(cosT, (1, 32))
    sq = np.tile(sinT, (1, 32))
    ck_ = np.tile(cosT, (1, 8))
    sk_ = np.tile(sinT, (1, 8))

    def rope_core(a, bb, ctab, stab):
        N = a.shape[1]
        ims = [{"p": aa, "q": cc, "r": bb_, "t": ss} for aa, bb_, cc, ss in
               zip(shard_rows(a), shard_rows(bb), shard_rows(ctab), shard_rows(stab))]
        outs = run_stage(("muladd", 1024, N, True), lambda: build_muladd(1024, N, True), ims,
                         ["o"], np_muladd_sub, ["p", "q", "r", "t"])
        oe = cat_rows(outs, "o")
        ims = [{"p": bb_, "q": cc, "r": aa, "t": ss} for aa, bb_, cc, ss in
               zip(shard_rows(a), shard_rows(bb), shard_rows(ctab), shard_rows(stab))]
        outs = run_stage(("muladd", 1024, N, False), lambda: build_muladd(1024, N, False), ims,
                         ["o"], np_muladd_add, ["p", "q", "r", "t"])
        oo = cat_rows(outs, "o")
        return oe, oo

    def rope_interleaved(x, ctab, stab, nh):
        a = np.ascontiguousarray(x[:, 0::2])
        bb = np.ascontiguousarray(x[:, 1::2])
        oe, oo = rope_core(a, bb, ctab, stab)
        r = np.empty_like(x)
        r[:, 0::2] = oe
        r[:, 1::2] = oo
        return r

    rq0 = rope_interleaved(q0, cq, sq, 32)
    rk0 = rope_interleaved(k0, ck_, sk_, 8)

    k_pos = np.asarray(inp["L0_attn_k_pos"], np.float32).reshape(8, 2048)
    v_pos = np.asarray(inp["L0_attn_v_pos"], np.float32).reshape(8, 2048)
    k_w1 = np.asarray(inp["L0_attn_k_w1"], np.float32)
    v_w1 = np.asarray(inp["L0_attn_v_w1"], np.float32)
    k_b1 = np.asarray(inp["L0_attn_k_b1"], np.float32)
    v_b1 = np.asarray(inp["L0_attn_v_b1"], np.float32)
    k_w2 = np.asarray(inp["L0_attn_k_w2"], np.float32)
    v_w2 = np.asarray(inp["L0_attn_v_w2"], np.float32)
    k_b2 = np.asarray(inp["L0_attn_k_b2"], np.float32)
    v_b2 = np.asarray(inp["L0_attn_v_b2"], np.float32)

    A_ = np.concatenate([k_pos.T, np.ones((1, 8), np.float32)], axis=0)
    Bk = np.concatenate([k_w1.T, k_b1[None, :]], axis=0)
    outs = mmT_launch("kposb", [A_] * NCORES, Bk, 2049, 8, 2048, "none")
    b1k = outs[0]["C"]
    A_ = np.concatenate([v_pos.T, np.ones((1, 8), np.float32)], axis=0)
    Bv = np.concatenate([v_w1.T, v_b1[None, :]], axis=0)
    outs = mmT_launch("kposb", [A_] * NCORES, Bv, 2049, 8, 2048, "none")
    b1v = outs[0]["C"]

    def ckin(x):
        return x.reshape(2, 256, 16, 8, 128).transpose(0, 3, 1, 2, 4).reshape(2, 8, 256, 2048)

    cki = ckin(k0)
    cvi = ckin(v0)
    onesc = np.ones((1, 512), np.float32)

    def mlp(ci, b1, w1, w2, b2):
        A_l = []
        for c in range(NCORES):
            a = np.concatenate([ci[0, c], ci[1, c]], axis=0).T
            A_l.append(np.concatenate([a, onesc], axis=0))
        B_l = [np.concatenate([w1.T, b1[c][None, :]], axis=0) for c in range(NCORES)]
        outs = run_stage(("mmT", 2176, 512, 2048, "relu"), lambda: build_mmT(2176, 512, 2048, "relu"),
                         [{"A": padk(A_l[c], 2176), "B": padk(B_l[c], 2176)} for c in range(NCORES)],
                         ["C"], lambda A, B: np_mmT(A, B, "relu"), ["A", "B"])
        MID = [o["C"] for o in outs]
        A2 = [np.concatenate([m.T, onesc], axis=0) for m in MID]
        B2 = np.concatenate([w2.T, b2[None, :]], axis=0)
        outs = run_stage(("mmT", 2176, 512, 128, "none"), lambda: build_mmT(2176, 512, 128, "none"),
                         [{"A": padk(A2[c], 2176), "B": padk(B2, 2176)} for c in range(NCORES)],
                         ["C"], lambda A, B: np_mmT(A, B, "none"), ["A", "B"])
        return [o["C"] for o in outs]

    CK = mlp(cki, b1k, k_w1, k_w2, k_b2)
    CVv = mlp(cvi, b1v, v_w1, v_w2, v_b2)
    mem_kv = np.asarray(inp["L0_attn_mem_kv"], np.float32)

    cmask = np.zeros((4096, 257), np.float32)
    ii = np.arange(4096)[:, None]
    jj = np.arange(256)[None, :]
    cmask[:, 0:256] = np.where(16 * jj + 15 < ii, 0.0, NEG)
    ident_bf = np.eye(128).astype(BF)

    def unit_io(b, h):
        qT = np.ascontiguousarray(q0[b * 4096:(b + 1) * 4096, h * 512:(h + 1) * 512].T)
        ck_u = CK[h][b * 256:(b + 1) * 256]
        cv_u = CVv[h][b * 256:(b + 1) * 256]
        ckT = np.concatenate([ck_u.T, mem_kv[0, h, 0][:, None]], axis=1)
        cv384 = np.zeros((384, 132), np.float32)
        cv384[0:256, 0:128] = cv_u
        cv384[256, 0:128] = mem_kv[1, h, 0]
        cv384[0:257, 128] = 1.0
        return qT, ckT, cv384

    PVCs = {}
    BMs = {}
    for ul in range(2):
        ims = []
        for c in range(NCORES):
            b_, h_ = ul, c
            qT, ckT, cv384 = unit_io(b_, h_)
            ims.append({"QT": qT, "CKT": ckT, "CV": cv384, "CM": cmask, "IDE": ident_bf})
        outs = run_stage(("csim",), build_csim, ims, ["PVC", "BM"],
                         np_csim, ["QT", "CKT", "CV", "CM"])
        for c in range(NCORES):
            PVCs[(ul, c)] = outs[c]["PVC"]
            BMs[(ul, c)] = outs[c]["BM"]

    kk = np.arange(4096)[:, None]
    qq = np.arange(4096)[None, :]
    ownT = (((kk // 16) == (qq // 16)) & (kk <= qq)).astype(BF)
    wmT = ((kk <= qq) & (kk >= qq - 64)).astype(BF)

    OUT0 = np.zeros((8192, 4096), np.float32)
    for ul in range(2):
        ims = []
        for c in range(NCORES):
            b_, h_ = ul, c
            rqT = np.ascontiguousarray(rq0[b_ * 4096:(b_ + 1) * 4096, h_ * 512:(h_ + 1) * 512].T)
            rkT = np.ascontiguousarray(rk0[b_ * 4096:(b_ + 1) * 4096, h_ * 128:(h_ + 1) * 128].T)
            vu = np.ascontiguousarray(v0[b_ * 4096:(b_ + 1) * 4096, h_ * 128:(h_ + 1) * 128])
            bmx = np.repeat(BMs[(ul, c)].T, 16, axis=0).astype(BF)
            ims.append({"RQT": rqT, "RKT": rkT, "V": vu, "BMX": bmx, "OWNT": ownT, "WMT": wmT})
        outs = run_stage(("fine",), build_fine, ims, ["OF", "OW", "ZF", "ZW"],
                         np_fine, ["RQT", "RKT", "V", "BMX", "OWNT", "WMT"])
        ims2 = []
        for c in range(NCORES):
            b_, h_ = ul, c
            ofT = outs[c]["OF"]
            owT = outs[c]["OW"]
            ofn = np.concatenate([ofT[h * 128:(h + 1) * 128, :].T for h in range(4)], axis=1)
            ownn = np.concatenate([owT[h * 128:(h + 1) * 128, :].T for h in range(4)], axis=1)
            g_u = GATES[b_ * 4096:(b_ + 1) * 4096, 12 * h_:12 * h_ + 12]
            ims2.append({"PVC": PVCs[(ul, c)], "OFN": ofn, "OWN": ownn,
                         "ZFn": outs[c]["ZF"].T.copy(), "ZWn": outs[c]["ZW"].T.copy(), "G": g_u})
        outs2 = run_stage(("combine0",), build_combine0, ims2, ["O"],
                          np_combine0, ["PVC", "OFN", "OWN", "ZFn", "ZWn", "G"])
        for c in range(NCORES):
            OUT0[ul * 4096:(ul + 1) * 4096, c * 512:(c + 1) * 512] = outs2[c]["O"]

    w_o = np.asarray(inp["L0_attn_w_o"], np.float32)
    OUT0T = np.ascontiguousarray(OUT0.T)
    A_l = [np.concatenate([OUT0T[:, c * 1024:(c + 1) * 1024], np.zeros((1, 1024), np.float32)], axis=0)
           for c in range(NCORES)]
    Bwo = np.concatenate([w_o.T, np.zeros((1, 4096), np.float32)], axis=0)
    outs = mmT_launch("wo", A_l, Bwo, 4097, 1024, 4096, "none")
    S0 = cat_rows(outs, "C")

    U1 = np.asarray(inp["L0_ffn_U1"], np.float32)
    V1 = np.asarray(inp["L0_ffn_V1"], np.float32)
    U2 = np.asarray(inp["L0_ffn_U2"], np.float32)
    V2 = np.asarray(inp["L0_ffn_V2"], np.float32)
    b1f = np.asarray(inp["L0_ffn_b1"], np.float32)
    b2f = np.asarray(inp["L0_ffn_b2"], np.float32)

    A_l = [np.concatenate([XN0T[:, c * 1024:(c + 1) * 1024], np.zeros((1, 1024), np.float32)], axis=0)
           for c in range(NCORES)]
    outs = mmT_launch("P", A_l, np.concatenate([U1, np.zeros((1, 1024), np.float32)], axis=0),
                      4097, 1024, 1024, "none")
    Pm = cat_rows(outs, "C")
    PT = np.ascontiguousarray(Pm.T)
    onesr = np.ones((1, 8192), np.float32)
    A_full = np.concatenate([PT, onesr], axis=0)
    B_l = []
    for c in range(NCORES):
        gsl = slice(1376 * c, 1376 * (c + 1))
        usl = slice(11008 + 1376 * c, 11008 + 1376 * (c + 1))
        Bc = np.concatenate([V1[:, gsl], V1[:, usl]], axis=1)
        bc = np.concatenate([b1f[gsl], b1f[usl]])[None, :]
        B_l.append(np.concatenate([Bc, bc], axis=0))
    outs = run_stage(("mmT", 1152, 8192, 2752, "none"), lambda: build_mmT(1152, 8192, 2752, "none"),
                     [{"A": padk(A_full, 1152), "B": padk(B_l[c], 1152)} for c in range(NCORES)],
                     ["C"], lambda A, B: np_mmT(A, B, "none"), ["A", "B"])
    Hs = [o["C"] for o in outs]

    outs = run_stage(("swiglu", 8192, 1376), lambda: build_swiglu(8192, 1376),
                     [{"H": Hs[c]} for c in range(NCORES)], ["O"], np_swiglu, ["H"])
    ACT = np.concatenate([o["O"] for o in outs], axis=1)

    ACTT = np.ascontiguousarray(ACT.T)
    A_l = [np.concatenate([ACTT[:, c * 1024:(c + 1) * 1024], np.ones((1, 1024), np.float32)], axis=0)
           for c in range(NCORES)]
    outs = mmT_launch("Z", A_l, np.concatenate([U2, np.zeros((1, 1024), np.float32)], axis=0),
                      11009, 1024, 1024, "none")
    Zm = cat_rows(outs, "C")
    ZT = np.ascontiguousarray(Zm.T)
    A_l = [np.concatenate([ZT[:, c * 1024:(c + 1) * 1024], np.ones((1, 1024), np.float32)], axis=0)
           for c in range(NCORES)]
    Bv2 = np.concatenate([V2, b2f[None, :]], axis=0)
    outs = mmT_launch("ffnout", A_l, Bv2, 1025, 1024, 4096, "silu")
    G0 = cat_rows(outs, "C")

    outs = run_stage(("ew3", 1024, 4096), lambda: build_ew3(1024, 4096),
                     [{"X": x, "Aa": a, "Bb": g} for x, a, g in
                      zip(shard_rows(X0), shard_rows(S0), shard_rows(G0))],
                     ["O"], np_ew3, ["X", "Aa", "Bb"])
    X1 = cat_rows(outs, "O")

    outs = run_stage(("ln", 1024, 4096), lambda: build_ln(1024, 4096),
                     [{"X": s} for s in shard_rows(X1)], ["Y"], np_ln, ["X"])
    XN1 = cat_rows(outs, "Y")
    XN1T = np.ascontiguousarray(XN1.T)

    wq = np.asarray(inp["L1_attn_wq"], np.float32)
    bq = np.asarray(inp["L1_attn_bq"], np.float32)
    wk = np.asarray(inp["L1_attn_wk"], np.float32)
    bk = np.asarray(inp["L1_attn_bk"], np.float32)
    wv = np.asarray(inp["L1_attn_wv"], np.float32)
    bv = np.asarray(inp["L1_attn_bv"], np.float32)
    A_l = [np.concatenate([XN1T[:, c * 1024:(c + 1) * 1024], ones_row], axis=0) for c in range(NCORES)]
    B1 = np.concatenate([np.concatenate([wq, wk, wv], axis=1),
                         np.concatenate([bq, bk, bv])[None, :]], axis=0)
    outs = mmT_launch("qkv", A_l, B1, 4097, 1024, 6144, "none")
    QKV1 = cat_rows(outs, "C")
    q1, k1, v1 = QKV1[:, 0:4096], QKV1[:, 4096:5120], QKV1[:, 5120:6144]

    def rope_half(x, nh, ctab, stab):
        xh = x.reshape(8192, nh, 2, 64)
        a = np.ascontiguousarray(xh[:, :, 0, :].reshape(8192, nh * 64))
        bb = np.ascontiguousarray(xh[:, :, 1, :].reshape(8192, nh * 64))
        oe, oo = rope_core(a, bb, ctab, stab)
        r = np.empty((8192, nh, 2, 64), np.float32)
        r[:, :, 0, :] = oe.reshape(8192, nh, 64)
        r[:, :, 1, :] = oo.reshape(8192, nh, 64)
        return r.reshape(8192, nh * 128)

    rq1 = rope_half(q1, 32, cq, sq)
    rk1 = rope_half(k1, 8, ck_, sk_)

    OT1 = np.zeros((8192, 4096), np.float32)
    Z1 = np.zeros((8192, 32), np.float32)
    for ul in range(2):
        ims = []
        for c in range(NCORES):
            b_, h_ = ul, c
            rqT = np.ascontiguousarray(rq1[b_ * 4096:(b_ + 1) * 4096, h_ * 512:(h_ + 1) * 512].T)
            rkT = np.ascontiguousarray(rk1[b_ * 4096:(b_ + 1) * 4096, h_ * 128:(h_ + 1) * 128].T)
            vu = np.ascontiguousarray(v1[b_ * 4096:(b_ + 1) * 4096, h_ * 128:(h_ + 1) * 128])
            ims.append({"RQT": rqT, "RKT": rkT, "V": vu})
        outs = run_stage(("l1attn",), build_l1attn, ims, ["OT", "Z"],
                         np_l1attn, ["RQT", "RKT", "V"])
        for c in range(NCORES):
            ot = outs[c]["OT"]
            for h in range(4):
                OT1[ul * 4096:(ul + 1) * 4096, (c * 4 + h) * 128:(c * 4 + h + 1) * 128] = \
                    ot[h * 128:(h + 1) * 128, :].T
            Z1[ul * 4096:(ul + 1) * 4096, c * 4:(c + 1) * 4] = outs[c]["Z"].T

    W_base = np.asarray(inp["L1_ffn_W_base"], np.float32)
    b_base = np.asarray(inp["L1_ffn_b_base"], np.float32)
    enc0 = np.asarray(inp["L1_ffn_enc0"], np.float32)
    enc1 = np.asarray(inp["L1_ffn_enc1"], np.float32)
    coefs0 = np.asarray(inp["L1_ffn_coefs0"], np.float32)
    coefs1 = np.asarray(inp["L1_ffn_coefs1"], np.float32)
    mask0 = np.asarray(inp["L1_ffn_mask0"]).astype(np.float32)
    mask1 = np.asarray(inp["L1_ffn_mask1"]).astype(np.float32)

    Cc = np.concatenate([np.concatenate([coefs0[e], coefs1[e]], axis=1) for e in range(3)], axis=1)
    Mm = np.concatenate([np.concatenate([mask0[e], mask1[e]], axis=1) for e in range(3)], axis=1)
    outs = run_stage(("masksum", 4096, 40), lambda: build_masksum(4096, 40),
                     [{"Cc": Cc, "Mm": Mm}] * NCORES, ["O"],
                     lambda C, M: np_masksum(C, M, 40), ["Cc", "Mm"])
    W01 = outs[0]["O"]

    A_l = [np.concatenate([XN1T[:, c * 1024:(c + 1) * 1024], np.zeros((1, 1024), np.float32)], axis=0)
           for c in range(NCORES)]
    Benc = np.concatenate([np.concatenate([enc0.T, enc1.T], axis=1), np.zeros((1, 40), np.float32)], axis=0)
    outs = mmT_launch("e01", A_l, Benc, 4097, 1024, 40, "none")
    E01 = cat_rows(outs, "C")

    Tt = _build_T().T.copy()
    B_l = [np.ascontiguousarray(Tt[:, c * 512:(c + 1) * 512]) for c in range(NCORES)]
    outs = run_stage(("mmT", 4096, 40, 512, "none"), lambda: build_mmT(4096, 40, 512, "none"),
                     [{"A": W01, "B": B_l[c]} for c in range(NCORES)], ["C"],
                     lambda A, B: np_mmT(A, B, "none"), ["A", "B"])
    A01sT = np.concatenate([o["C"] for o in outs], axis=1)

    A_l = [np.concatenate([XN1T[:, c * 1024:(c + 1) * 1024], ones_row], axis=0) for c in range(NCORES)]
    Bb_ = np.concatenate([W_base.T, b_base[None, :]], axis=0)
    outs = mmT_launch("wo", A_l, Bb_, 4097, 1024, 4096, "none")
    BASE = cat_rows(outs, "C")

    E01T = np.ascontiguousarray(E01.T)
    A_l = [np.ascontiguousarray(E01T[:, c * 1024:(c + 1) * 1024]) for c in range(NCORES)]
    outs = run_stage(("mmT", 128, 1024, 4096, "none"), lambda: build_mmT(128, 1024, 4096, "none"),
                     [{"A": padk(A_l[c], 128), "B": padk(A01sT, 128)} for c in range(NCORES)],
                     ["C"], lambda A, B: np_mmT(A, B, "none"), ["A", "B"])
    UPD = cat_rows(outs, "C")

    outs = run_stage(("g1silu", 1024, 4096), lambda: build_g1silu(1024, 4096),
                     [{"X": x, "U": u} for x, u in zip(shard_rows(BASE), shard_rows(UPD))],
                     ["O"], np_g1silu, ["X", "U"])
    G1 = cat_rows(outs, "O")

    outs = run_stage(("x2comb", 1024), lambda: build_x2comb(1024),
                     [{"X1": x, "O1": o, "Z1": z, "G1": g} for x, o, z, g in
                      zip(shard_rows(X1), shard_rows(OT1), shard_rows(Z1), shard_rows(G1))],
                     ["O"], np_x2comb, ["X1", "O1", "Z1", "G1"])
    X2 = cat_rows(outs, "O")
    return X2.reshape(2, 4096, 4096).astype(np.float32)
